# version 10
# speedup vs baseline: 1.0514x; 1.0007x over previous
import numpy as np
import ml_dtypes
import concourse.bass as bass
import concourse.mybir as mybir
from concourse.bass_utils import run_bass_kernel_spmd
from contextlib import ExitStack

F32 = mybir.dt.float32
F16 = mybir.dt.float16
BF16 = mybir.dt.bfloat16
I32 = mybir.dt.int32
AF = mybir.ActivationFunctionType
ALU = mybir.AluOpType
BF = ml_dtypes.bfloat16

B, S, E = 2, 8192, 2048
H, G, Dh = 16, 4, 128
NBLK, BLK = 8, 1024
TOK = 2 * BLK
FC = E // 128
FI = 5632
OGU = 2 * FI
NH, HD, RK, NS, KC = 2, 2048, 128, 16, 4
IM = NH * HD
LN_EPS = 1e-5
RMS_EPS = 1e-6
ISD = float(1.0 / np.sqrt(Dh))

DEBUG_PHASES = 99
DEBUG_DUMPS = ()
FFN_STAGE = 4



def host_prepare(inp):
    pr = {}
    x = np.asarray(inp["L0_attn_a_x"])
    for b in range(B):
        pr[f"xT_{b}"] = np.ascontiguousarray(x[b].T)
    for q in range(4):
        bA, bB = q, 7 - q
        for b in range(B):
            xt = pr[f"xT_{b}"]
            pr[f"xTown_{b}_{q}"] = np.ascontiguousarray(
                np.concatenate([xt[:, bA * BLK:(bA + 1) * BLK],
                                xt[:, bB * BLK:(bB + 1) * BLK]], axis=1))
    pr["WqT"] = np.ascontiguousarray(np.asarray(inp["L0_attn_a_Wq"]).transpose(1, 0, 2).reshape(E, H * Dh))
    pr["WkT"] = np.ascontiguousarray(np.asarray(inp["L0_attn_a_Wk"]).transpose(1, 0, 2).reshape(E, G * Dh))
    pr["WvT"] = np.ascontiguousarray(np.asarray(inp["L0_attn_a_Wv"]).transpose(1, 0, 2).reshape(E, G * Dh))
    pr["bqT"] = np.ascontiguousarray(np.asarray(inp["L0_attn_a_bq"]).T)
    pr["bkT"] = np.ascontiguousarray(np.asarray(inp["L0_attn_a_bk"]).T)
    pr["bv"] = np.ascontiguousarray(np.asarray(inp["L0_attn_a_bv"]).reshape(1, G * Dh))
    pr["inT"] = np.ascontiguousarray(np.asarray(inp["L0_attn_b_in_proj_w"]).T)
    pr["outT"] = np.ascontiguousarray(np.asarray(inp["L0_attn_b_out_proj_w"]).T)
    pr["xpT"] = np.ascontiguousarray(np.asarray(inp["L0_attn_b_x_proj_weight"]).transpose(0, 2, 1))
    pr["dtT"] = np.ascontiguousarray(np.asarray(inp["L0_attn_b_dt_proj_weight"]).transpose(0, 2, 1))
    pr["dtb"] = np.ascontiguousarray(np.asarray(inp["L0_attn_b_dt_proj_bias"]).reshape(2, 16, 128).transpose(2, 0, 1).reshape(128, 32))
    pr["convw"] = np.ascontiguousarray(np.asarray(inp["L0_attn_b_conv_w"]))
    pr["convb"] = np.ascontiguousarray(np.asarray(inp["L0_attn_b_conv_b"]).reshape(32, 128).T)
    pr["Alog"] = np.ascontiguousarray(np.asarray(inp["L0_attn_b_A_log"]))
    pr["Dmix"] = np.ascontiguousarray(np.asarray(inp["L0_attn_b_D"]).reshape(32, 128).T)
    pr["gupT"] = np.ascontiguousarray(np.asarray(inp["L0_ffn_gate_up_packed"]).T)
    pr["dnpT"] = np.ascontiguousarray(np.asarray(inp["L0_ffn_down_packed"]).T)
    pr["gus"] = np.ascontiguousarray(np.asarray(inp["L0_ffn_gate_up_scales"]))
    pr["dns"] = np.ascontiguousarray(np.asarray(inp["L0_ffn_down_scales"]))
    pr["identb"] = np.eye(128, dtype=np.float32).astype(BF)
    pr["identf"] = np.eye(128, dtype=np.float32)
    col = np.arange(BLK)[None, :]
    row = np.arange(128)[:, None]
    for q in range(4):
        m = np.zeros((2, 8, 8, 128, BLK), dtype=np.float32)
        for bs, blk in enumerate((q, 7 - q)):
            for i in range(8):
                qglob = blk * BLK + i * 128 + row
                for j in range(8):
                    m[bs, i, j] = (j * BLK + col) <= qglob
        mt = m.reshape(2, 8, 8, 128, 8, 128).transpose(0, 1, 2, 5, 4, 3).reshape(2, 8, 8, 128, BLK)
        pr[f"mask_{q}"] = np.ascontiguousarray(mt).astype(BF)
        mr = np.zeros((2, 8, 8, BLK), dtype=np.float32)
        for bs, blk in enumerate((q, 7 - q)):
            for i in range(8):
                ext = blk * BLK + 128 * (i + 1)
                for j in range(8):
                    mr[bs, i, j] = np.where(j * BLK + np.arange(BLK) >= ext, -30000.0, 0.0)
        pr[f"maskrow_{q}"] = mr.astype(BF)
        for b in range(B):
            selA = np.zeros((1, 16), np.float32)
            selB = np.zeros((1, 16), np.float32)
            selA[0, 8 * b + q] = 1.0
            selB[0, 8 * b + 7 - q] = 1.0
            pr[f"selA_{b}_{q}"] = selA
            pr[f"selB_{b}_{q}"] = selB
        for b in range(B):
            hA = np.zeros((1, 16), np.float32)
            hB = np.zeros((1, 16), np.float32)
            if q - 1 >= 0:
                hA[0, 8 * b + q - 1] = 1.0
            hB[0, 8 * b + 6 - q] = 1.0
            pr[f"halA_{b}_{q}"] = hA
            pr[f"halB_{b}_{q}"] = hB
    return pr


def per_core_inputs(c, pr):
    b, q = c // 4, c % 4
    d = {"xT": pr[f"xT_{b}"], "xTown": pr[f"xTown_{b}_{q}"], "mask": pr[f"mask_{q}"],
         "maskrow": pr[f"maskrow_{q}"],
         "selA": pr[f"selA_{b}_{q}"], "selB": pr[f"selB_{b}_{q}"],
         "halA": pr[f"halA_{b}_{q}"], "halB": pr[f"halB_{b}_{q}"]}
    for k in ("WqT", "WkT", "WvT", "bqT", "bkT", "bv", "inT", "outT", "xpT", "dtT", "dtb",
              "convw", "convb", "Alog", "Dmix", "gupT", "dnpT", "gus", "dns", "identb", "identf"):
        d[k] = pr[k]
    return d



class Prog:
    def __init__(self, nc, block, stack):
        self.nc, self.block, self._stack = nc, block, stack
        self._sems = {}
        self._free = []

    def sem(self, name):
        if name not in self._sems:
            if self._free:
                self._sems[name] = self._free.pop()
            else:
                self._sems[name] = [self._stack.enter_context(self.nc.semaphore(name)), 0]
            self._base = getattr(self, "_base", {})
            self._base[name] = self._sems[name][1]
        return self._sems[name][0]

    def lo(self, name, k=1):
        self.sem(name)
        return max(self._base.get(name, 0), self._sems[name][1] - k)

    def recycle(self):
        for name in list(self._sems):
            self._free.append(self._sems.pop(name))

    def cnt(self, name):
        self.sem(name)
        return self._sems[name][1]

    def dma(self, eng, qname, out, in_, **kw):
        eng.dma_start(out=out, in_=in_, **kw).then_inc(self.sem(qname), 16)
        self._sems[qname][1] += 16
        return self._sems[qname][1]

    def tag(self, instr, sname):
        instr.then_inc(self.sem(sname), 1)
        self._sems[sname][1] += 1
        return self._sems[sname][1]

    def wait(self, eng, sname, val=None):
        s = self.sem(sname)
        eng.wait_ge(s, self._sems[sname][1] if val is None else val)

    def chain(self, eng, instr, sname="rawsync"):
        c = self.tag(instr, sname)
        eng.wait_ge(self.sem(sname), c)


def build_program(debug_phases=99, debug_dumps=(), shared_cc=True):
    nc = bass.Bass("TRN2", target_bir_lowering=False, debug=False)
    dt = {}

    def din(name, shape, dtype):
        dt[name] = nc.dram_tensor(name, shape, dtype, kind="ExternalInput").ap()

    def dscr(name, shape, dtype):
        dt[name] = nc.dram_tensor(name, shape, dtype).ap()

    din("xT", [E, S], F32)
    din("xTown", [E, TOK], F32)
    din("mask", [2, 8, 8, 128, BLK], BF16)
    din("maskrow", [2, 8, 8, BLK], BF16)
    din("selA", [1, 16], F32)
    din("selB", [1, 16], F32)
    din("halA", [1, 16], F32)
    din("halB", [1, 16], F32)
    din("WqT", [E, H * Dh], F32)
    din("WkT", [E, G * Dh], F32)
    din("WvT", [E, G * Dh], F32)
    din("bqT", [128, H], F32)
    din("bkT", [128, G], F32)
    din("bv", [1, G * Dh], F32)
    din("inT", [E, 2 * IM], F32)
    din("outT", [IM, E], F32)
    din("xpT", [NH, HD, 160], F32)
    din("dtT", [NH, RK, HD], F32)
    din("dtb", [128, 32], F32)
    din("convw", [IM, KC], F32)
    din("convb", [128, 32], F32)
    din("Alog", [NH, HD, NS], F32)
    din("Dmix", [128, 32], F32)
    din("gupT", [256, OGU], I32)
    din("dnpT", [704, E], I32)
    din("gus", [16, OGU], F16)
    din("dns", [44, E], F16)
    din("identb", [128, 128], BF16)
    din("identf", [128, 128], F32)

    dt["out"] = nc.dram_tensor("out", [E, TOK], F32, kind="ExternalOutput").ap()

    dscr("kT_s", [G, 128, S], BF16)
    dscr("v_s", [G, S // 128, 128, 129], BF16)
    dscr("qT_s", [2, 8, H, 128, 128], BF16)
    dscr("x1T_s", [E, TOK], F32)
    dscr("x2T_s", [E, TOK], F32)
    dscr("x3T_s", [E, TOK], F32)
    dscr("xh_s", [E, TOK], BF16)
    dscr("wgu_s", [E, OGU], BF16)
    dscr("wdn_s", [FI, E], BF16)
    dscr("convx_s", [IM, TOK], BF16)
    dscr("gate_s", [IM, TOK], BF16)
    dscr("dt_s", [IM, TOK], BF16)
    dscr("bc_s", [NH, 32, TOK], BF16)
    dscr("y2_s", [IM, TOK], BF16)
    dscr("halo_in", [128, 2 * 48], BF16)
    if shared_cc:
        dt["halo_out"] = nc.dram_tensor("halo_out", [8 * 128, 2 * 48], BF16, addr_space="Shared").ap()
    else:
        dscr("halo_out", [8 * 128, 2 * 48], BF16)
    dscr("sum_in", [128, 2048], BF16)
    dscr("stin_s", [2, 128, 512], F32)
    dscr("dbg_s", [128, 4096], F32)
    if shared_cc:
        dt["sum_out"] = nc.dram_tensor("sum_out", [8 * 128, 2048], BF16, addr_space="Shared").ap()
    else:
        dscr("sum_out", [8 * 128, 2048], BF16)

    for nm in debug_dumps:
        t = dt[nm]
        dt["dump_" + nm] = nc.dram_tensor("dump_" + nm, list(t.shape), t.dtype, kind="ExternalOutput").ap()

    with ExitStack() as ctx:
        block = ctx.enter_context(nc.Block())
        p = Prog(nc, block, ctx)
        sb = {}
        scope = [ctx]

        uniq = [0]

        def S_(name, shape, dtype):
            uniq[0] += 1
            sb[name] = scope[0].enter_context(nc.sbuf_tensor(f"{name}_{uniq[0]}", shape, dtype))
            return sb[name]

        def P_(name, shape, dtype=F32):
            uniq[0] += 1
            sb[name] = scope[0].enter_context(nc.psum_tensor(f"{name}_{uniq[0]}", shape, dtype))
            return sb[name]

        S_("ones_b", [128, 1], BF16)
        S_("ones_row", [1, 128], F32)
        S_("ones_rowb", [1, 128], BF16)
        S_("id_b", [128, 128], BF16)
        S_("id_f", [128, 128], F32)

        @block.vector
        def _(v):
            v.memset(sb["ones_b"][:], 1.0)
            v.memset(sb["ones_rowb"][:], 1.0)
            p.tag(v.memset(sb["ones_row"][:], 1.0), "v_init")

        @block.sync
        def _(s):
            p.dma(s, "q_id", sb["id_b"][:], dt["identb"])
            p.dma(s, "q_id", sb["id_f"][:], dt["identf"])

        phases = [g for g in (globals().get(n) for n in
                   ("ph_kvq", "ph_attn", "ph_dequant", "ph_ffn1", "ph_mixfront", "ph_scan1",
                    "ph_combine", "ph_scan2", "ph_outproj", "ph_ffn2")) if g is not None]
        for i, ph in enumerate(phases):
            if i >= debug_phases:
                break
            with ExitStack() as phctx:
                scope[0] = phctx
                ph(p, dt, sb, S_, P_)
            scope[0] = ctx
            barrier(p)
        finish(p, dt, sb, debug_dumps)

    return nc, dt


def barrier(p):
    blk = p.block
    names = [n for n in p._sems if n not in ("qout", "qg")]

    def w(e):
        for nm in names:
            p.wait(e, nm)

    @blk.sync
    def _(s):
        w(s)

    @blk.scalar
    def _(sc):
        w(sc)

    @blk.vector
    def _(v):
        w(v)

    @blk.tensor
    def _(te):
        w(te)

    @blk.gpsimd
    def _(g):
        w(g)
    p.recycle()


def finish(p, dt, sb, debug_dumps):
    blk = p.block

    @blk.gpsimd
    def _(g):
        for nm in list(p._sems):
            if nm != "qg":
                p.wait(g, nm)
        for nm in debug_dumps:
            p.dma(g, "qg", dt["dump_" + nm], dt[nm])
        g.wait_ge(p.sem("qg"), p.cnt("qg"))

    @blk.sync
    def _(s):
        for nm in list(p._sems):
            if nm not in ("qout", "qg"):
                p.wait(s, nm)
        if not dt.get("__out_written__"):
            p.dma(s, "qout", dt["out"], dt["x1T_s"])
            s.wait_ge(p.sem("qout"), p.cnt("qout"))



def ph_kvq(p, dt, sb, S_, P_):
    nc, blk = p.nc, p.block
    TC = 512

    S_("x_f32", [128, FC * TC], F32)
    S_("x_bf", [128, FC * TC], BF16)
    S_("x_sq", [128, FC * TC], BF16)
    S_("x_h", [128, FC * TC], BF16)
    S_("stat", [1, 2 * TC], F32)
    S_("w_stage", [128, FC * 512], F32)
    S_("wbf", [128, FC * 1024], BF16)
    S_("bq_c", [128, H], F32)
    S_("bk_c", [128, G], F32)
    S_("bv_b", [128, G * Dh], F32)
    S_("kq_ev", [128, 512], BF16)
    S_("v_ev", [128, 4 * 129], BF16)
    P_("p_stat", [1, 2 * TC])
    P_("p_bcast", [128, 2 * TC])
    P_("p_k", [128, TC])
    P_("p_v", [128, G * Dh])

    @blk.vector
    def _(v):
        p.tag(v.memset(sb["v_ev"][:], 1.0), "v_init")

    @blk.sync
    def _(s):
        p.dma(s, "q_w", sb["w_stage"][:].rearrange("p (f d) -> p f d", f=FC),
              dt["WkT"].rearrange("(f p) d -> p f d", p=128))
        p.dma(s, "q_w", sb["bk_c"][:], dt["bkT"])
        p.dma(s, "q_w", sb["bq_c"][:], dt["bqT"])
        p.dma(s, "q_w", sb["bv_b"][:], dt["bv"].partition_broadcast(128))

    @blk.scalar
    def _(sc):
        p.wait(sc, "q_w")
        p.tag(sc.activation(out=sb["wbf"][:, 0:FC * 512], in_=sb["w_stage"][:, 0:FC * 512], func=AF.Copy), "c_w")

    @blk.sync
    def _(s):
        p.wait(s, "c_w")
        p.dma(s, "q_w", sb["w_stage"][:].rearrange("p (f d) -> p f d", f=FC),
              dt["WvT"].rearrange("(f p) d -> p f d", p=128))

    @blk.scalar
    def _(sc):
        p.wait(sc, "q_w")
        p.tag(sc.activation(out=sb["wbf"][:, FC * 512:FC * 1024], in_=sb["w_stage"][:, 0:FC * 512], func=AF.Copy), "c_w")


    @blk.vector
    def _(v):
        p.wait(v, "q_w")
        p.tag(v.tensor_scalar(out=sb["bq_c"][:], in0=sb["bq_c"][:], scalar1=ISD, scalar2=None,
                              op0=ALU.mult), "c_bq")

    for t in range(S // TC):
        _ln_chunk(p, dt, sb, src="xT", t=t, TC=TC)
        _kv_chunk(p, dt, sb, t=t, TC=TC)
    for t in range(TOK // TC):
        _ln_chunk(p, dt, sb, src="xTown", t=t, TC=TC)

        @blk.sync
        def _(s, t=t):
            p.wait(s, "v_xh")
            c = p.dma(s, "q_xh", dt["xh_s"].rearrange("(f p) s -> p f s", p=128)[:, :, t * TC:(t + 1) * TC],
                      sb["x_h"][:].rearrange("p (f s) -> p f s", f=FC))

        @blk.tensor
        def _(te, t=t):
            p.wait(te, "q_xh")
            p.tag(te.nop(), "pe_use_xh")
    for hg in range(4):
        @blk.sync
        def _(s, hg=hg):
            p.wait(s, "c_w")
            p.wait(s, "pe_q")
            p.dma(s, "q_w", sb["w_stage"][:].rearrange("p (f d) -> p f d", f=FC),
                  dt["WqT"][:, hg * 512:(hg + 1) * 512].rearrange("(f p) d -> p f d", p=128))

        @blk.scalar
        def _(sc, hg=hg):
            p.wait(sc, "q_w")
            p.tag(sc.activation(out=sb["wbf"][:, 0:FC * 512], in_=sb["w_stage"][:],
                                func=AF.Copy, scale=ISD), "c_w")
        for t in range(TOK // TC):
            @blk.sync
            def _(s, t=t, hg=hg):
                p.wait(s, "pe_q")
                p.wait(s, "q_xh")
                p.dma(s, "q_qx", sb["x_h"][:].rearrange("p (f s) -> p f s", f=FC),
                      dt["xh_s"].rearrange("(f p) s -> p f s", p=128)[:, :, t * TC:(t + 1) * TC])
            _q_chunk(p, dt, sb, t=t, TC=TC, hg=hg)


def _ln_chunk(p, dt, sb, src, t, TC):
    blk = p.block
    consumed = [p.cnt("v_xh_consumed")]

    @blk.sync
    def _(s):
        p.wait(s, "pe_use_xh")
        p.wait(s, "v_xh")
        p.dma(s, "q_x", sb["x_f32"][:].rearrange("p (f s) -> p f s", f=FC),
              dt[src].rearrange("(f p) s -> p f s", p=128)[:, :, t * TC:(t + 1) * TC])

    @blk.scalar
    def _(sc):
        p.wait(sc, "q_x")
        p.wait(sc, "pe_stat")
        for f in range(FC):
            p.tag(sc.activation(out=sb["x_bf"][:, f * TC:(f + 1) * TC],
                                in_=sb["x_f32"][:, f * TC:(f + 1) * TC], func=AF.Copy), "s_cast")
            p.tag(sc.activation(out=sb["x_sq"][:, f * TC:(f + 1) * TC],
                                in_=sb["x_f32"][:, f * TC:(f + 1) * TC], func=AF.Square), "s_cast")

    @blk.tensor
    def _(te):
        p.wait(te, "s_cast")
        p.wait(te, "v_init")
        p.wait(te, "v_stat")
        for f in range(FC):
            te.matmul(sb["p_stat"][0:1, 0:TC], sb["ones_b"][:], sb["x_bf"][:, f * TC:(f + 1) * TC],
                      start=(f == 0), stop=(f == FC - 1))
        for f in range(FC):
            i = te.matmul(sb["p_stat"][0:1, TC:2 * TC], sb["ones_b"][:], sb["x_sq"][:, f * TC:(f + 1) * TC],
                          start=(f == 0), stop=(f == FC - 1))
        p.tag(i, "pe_stat")

    @blk.vector
    def _(v):
        p.wait(v, "pe_stat")
        p.chain(v, v.tensor_scalar(out=sb["stat"][0:1, 0:TC], in0=sb["p_stat"][0:1, 0:TC],
                                   scalar1=1.0 / E, scalar2=None, op0=ALU.mult))
        v.tensor_scalar(out=sb["stat"][0:1, TC:2 * TC], in0=sb["p_stat"][0:1, TC:2 * TC],
                        scalar1=1.0 / E, scalar2=LN_EPS, op0=ALU.mult, op1=ALU.add)
        p.chain(v, v.tensor_tensor(out=sb["p_stat"][0:1, 0:TC], in0=sb["stat"][0:1, 0:TC],
                                   in1=sb["stat"][0:1, 0:TC], op=ALU.mult))
        p.tag(v.tensor_tensor(out=sb["stat"][0:1, TC:2 * TC], in0=sb["stat"][0:1, TC:2 * TC],
                              in1=sb["p_stat"][0:1, 0:TC], op=ALU.subtract), "v_stat")

    @blk.scalar
    def _(sc):
        p.wait(sc, "v_stat")
        p.chain(sc, sc.activation(out=sb["stat"][0:1, TC:2 * TC], in_=sb["stat"][0:1, TC:2 * TC],
                                  func=AF.Ln))
        p.tag(sc.activation(out=sb["stat"][0:1, TC:2 * TC], in_=sb["stat"][0:1, TC:2 * TC],
                            func=AF.Exp, scale=-0.5), "s_rstd")

    @blk.tensor
    def _(te):
        p.wait(te, "s_rstd")
        p.wait(te, "v_xh")
        te.matmul(sb["p_bcast"][:, 0:TC], sb["ones_row"][:], sb["stat"][0:1, 0:TC],
                  start=True, stop=True)
        p.tag(te.matmul(sb["p_bcast"][:, TC:2 * TC], sb["ones_row"][:], sb["stat"][0:1, TC:2 * TC],
                        start=True, stop=True), "pe_bcast")

    @blk.vector
    def _(v):
        p.wait(v, "pe_bcast")
        for f in range(FC):
            i = v.tensor_tensor(out=sb["x_f32"][:, f * TC:(f + 1) * TC],
                                in0=sb["x_f32"][:, f * TC:(f + 1) * TC],
                                in1=sb["p_bcast"][:, 0:TC], op=ALU.subtract)
        p.chain(v, i)
        for f in range(FC):
            p.tag(v.tensor_tensor(out=sb["x_h"][:, f * TC:(f + 1) * TC],
                                  in0=sb["x_f32"][:, f * TC:(f + 1) * TC],
                                  in1=sb["p_bcast"][:, TC:2 * TC], op=ALU.mult), "v_xh")


def _kv_chunk(p, dt, sb, t, TC):
    blk = p.block

    for g in range(G):
        @blk.tensor
        def _(te, g=g):
            p.wait(te, "v_xh")
            if g == 0:
                p.wait(te, "s_kev")
            else:
                p.wait(te, "s_kev")
            for f in range(FC):
                i = te.matmul(sb["p_k"][:], sb["wbf"][:, f * 512 + g * 128: f * 512 + (g + 1) * 128],
                              sb["x_h"][:, f * TC:(f + 1) * TC], start=(f == 0), stop=(f == FC - 1))
            p.tag(i, "pe_k")
            if g == G - 1:
                p.tag(te.nop(), "pe_use_xh")

        @blk.scalar
        def _(sc, g=g):
            p.wait(sc, "pe_k")
            p.wait(sc, "q_kv")
            p.tag(sc.activation(out=sb["kq_ev"][:, 0:TC], in_=sb["p_k"][:], func=AF.Identity,
                                bias=sb["bk_c"][:, g:g + 1]), "s_kev")

        @blk.sync
        def _(s, g=g, t=t):
            p.wait(s, "s_kev")
            p.dma(s, "q_kv", dt["kT_s"][g, :, t * TC:(t + 1) * TC], sb["kq_ev"][:, 0:TC])

    for tt in range(TC // 128):
        @blk.tensor
        def _(te, tt=tt):
            p.wait(te, "v_xh")
            p.wait(te, "v_vev")
            for f in range(FC):
                i = te.matmul(sb["p_v"][:], sb["x_h"][:, f * TC + tt * 128: f * TC + (tt + 1) * 128],
                              sb["wbf"][:, FC * 512 + f * 512:FC * 512 + (f + 1) * 512], start=(f == 0), stop=(f == FC - 1))
            p.tag(i, "pe_v")
            if tt == TC // 128 - 1:
                p.tag(te.nop(), "pe_use_xh")

        @blk.vector
        def _(v, tt=tt):
            p.wait(v, "pe_v")
            p.wait(v, "q_kv")
            for g in range(G):
                p.tag(v.tensor_tensor(out=sb["v_ev"][:, g * 129:g * 129 + 128],
                                      in0=sb["p_v"][:, g * 128:(g + 1) * 128],
                                      in1=sb["bv_b"][:, g * 128:(g + 1) * 128], op=ALU.add), "v_vev")

        @blk.sync
        def _(s, tt=tt, t=t):
            p.wait(s, "v_vev")
            gt = t * (TC // 128) + tt
            p.dma(s, "q_kv", dt["v_s"][:, gt, :, :].transpose([1, 0, 2]),
                  sb["v_ev"][:].rearrange("p (g c) -> p g c", g=G))


def _q_chunk(p, dt, sb, t, TC, hg):
    blk = p.block

    for hh in range(4):
        h = hg * 4 + hh

        @blk.tensor
        def _(te, hh=hh):
            p.wait(te, "q_qx")
            p.wait(te, "c_w")
            p.wait(te, "s_qev")
            p.wait(te, "s_kev")
            for f in range(FC):
                i = te.matmul(sb["p_k"][:], sb["wbf"][:, f * 512 + hh * 128: f * 512 + (hh + 1) * 128],
                              sb["x_h"][:, f * TC:(f + 1) * TC], start=(f == 0), stop=(f == FC - 1))
            p.tag(i, "pe_q")

        @blk.scalar
        def _(sc, h=h):
            p.wait(sc, "pe_q")
            p.wait(sc, "q_kv")
            p.tag(sc.activation(out=sb["kq_ev"][:, 0:TC], in_=sb["p_k"][:], func=AF.Identity,
                                bias=sb["bq_c"][:, h:h + 1]), "s_qev")

        @blk.sync
        def _(s, h=h, t=t):
            p.wait(s, "s_qev")
            blkslot = t // 2
            for i2 in range(TC // 128):
                i = (t % 2) * 4 + i2
                p.dma(s, "q_kv", dt["qT_s"][blkslot, i, h, :, :], sb["kq_ev"][:, i2 * 128:(i2 + 1) * 128])





def ph_attn(p, dt, sb, S_, P_):
    blk = p.block
    S_("at_kT", [128, S], BF16)
    S_("at_v", [128, (S // 128) * 129], BF16)
    S_("at_mk", [128, 2 * 8 * BLK], BF16)
    S_("at_mr", [1, 2 * 8 * BLK], BF16)
    S_("at_qt", [128, 2 * 4 * 128], BF16)
    S_("at_P", [128, 2 * BLK], BF16)
    S_("at_PT", [128, 2 * BLK], BF16)
    S_("at_acc", [128, 129], F32)
    S_("at_nm", [128, 2], F32)
    S_("at_dm", [128, 2], F32)
    S_("at_rs", [128, 2], F32)
    S_("at_rf", [128, 1], F32)
    S_("at_att", [128, 128], F32)
    S_("at_xo", [128, 128], F32)
    S_("at_x1", [128, 128], F32)
    P_("at_ps", [128, 2 * BLK])
    P_("at_pt", [128, 2 * BLK], BF16)
    P_("at_pav", [128, 129])
    P_("at_ptr", [128, 128])

    ji = [0]
    snaps = {}

    for g in range(G):
        @blk.sync
        def _(s, g=g):
            p.wait(s, "pe_av")
            p.dma(s, "q_at", sb["at_kT"][:], dt["kT_s"][g])
            p.dma(s, "q_at", sb["at_v"][:].rearrange("p (t c) -> p t c", c=129),
                  dt["v_s"][g].transpose([1, 0, 2]))

        for pi in range(16):
            blkslot, i = pi // 8, pi % 8
            nj = 4 if blkslot == 0 else 8
            gp = g * 16 + pi
            pp = gp % 2

            @blk.sync
            def _(s, blkslot=blkslot, i=i, nj=nj, gp=gp, pp=pp):
                if gp - 2 in snaps:
                    p.wait(s, "s_ptev", snaps[gp - 2][0])
                    p.wait(s, "pe_s", snaps[gp - 2][1])
                p.dma(s, "q_at", sb["at_mk"][:, pp * 8 * BLK: pp * 8 * BLK + nj * BLK]
                      .rearrange("p (j c) -> p j c", c=BLK),
                      dt["mask"][blkslot, i, 0:nj].transpose([1, 0, 2]))
                p.dma(s, "q_at", sb["at_mr"][0:1, pp * 8 * BLK: pp * 8 * BLK + nj * BLK],
                      dt["maskrow"][blkslot, i, 0:nj].rearrange("j c -> (j c)")[None, :])
                p.dma(s, "q_at", sb["at_qt"][:, pp * 512:(pp + 1) * 512]
                      .rearrange("p (h c) -> p h c", c=128),
                      dt["qT_s"][blkslot, i, 4 * g:4 * (g + 1)].transpose([1, 0, 2]))

            for hh in range(4):
                @blk.vector
                def _(v, ji0=ji[0]):
                    p.wait(v, "v_att")
                    p.wait(v, "s_pexp")
                    v.memset(sb["at_acc"][:], 0.0)
                    p.tag(v.memset(sb["at_nm"][:, ji0 % 2:ji0 % 2 + 1], 3e38), "v_accinit")

                for j in range(nj):
                    jj = ji[0]
                    ji[0] += 1
                    par = jj % 2
                    cur, nxt = jj % 2, (jj + 1) % 2
                    PS = sb["at_ps"][:, par * BLK:(par + 1) * BLK]
                    PP = sb["at_P"][:, par * BLK:(par + 1) * BLK]
                    PT = sb["at_PT"][:, par * BLK:(par + 1) * BLK]
                    PTp = sb["at_pt"][:, par * BLK:(par + 1) * BLK]
                    RS = sb["at_rs"][:, par:par + 1]
                    DM = sb["at_dm"][:, par:par + 1]

                    @blk.tensor
                    def _(te, hh=hh, j=j, jj=jj, pp=pp, PS=PS):
                        p.wait(te, "q_at")
                        p.wait(te, "v_init")
                        p.wait(te, "s_pexp", p.lo("s_pexp", 1))
                        p.wait(te, "v_rm", p.lo("v_rm", 1))
                        qtile = sb["at_qt"][:, pp * 512 + hh * 128: pp * 512 + (hh + 1) * 128]
                        mrow = sb["at_mr"][0:1, pp * 8 * BLK + j * BLK: pp * 8 * BLK + (j + 1) * BLK]
                        te.matmul(PS[:, 0:512], qtile, sb["at_kT"][:, j * BLK:j * BLK + 512],
                                  start=True, stop=False)
                        te.matmul(PS[:, 0:512], sb["ones_rowb"][:], mrow[:, 0:512],
                                  start=False, stop=True)
                        te.matmul(PS[:, 512:BLK], qtile, sb["at_kT"][:, j * BLK + 512:(j + 1) * BLK],
                                  start=True, stop=False)
                        p.tag(te.matmul(PS[:, 512:BLK], sb["ones_rowb"][:], mrow[:, 512:BLK],
                                        start=False, stop=True), "pe_s")

                    @blk.vector
                    def _(v, jj=jj, cur=cur, nxt=nxt, PS=PS, DM=DM):
                        p.wait(v, "pe_s")
                        p.wait(v, "v_accinit")
                        p.chain(v, v.reduce_max(out=DM, in_=PS, axis=mybir.AxisListType.X, negate=True))
                        p.chain(v, v.tensor_tensor(out=sb["at_nm"][:, nxt:nxt + 1],
                                                   in0=sb["at_nm"][:, cur:cur + 1], in1=DM, op=ALU.min))
                        p.tag(v.tensor_tensor(out=DM, in0=sb["at_nm"][:, nxt:nxt + 1],
                                              in1=sb["at_nm"][:, cur:cur + 1], op=ALU.subtract), "v_rm")

                    @blk.scalar
                    def _(sc, jj=jj, nxt=nxt, PS=PS, PP=PP, RS=RS, DM=DM):
                        p.wait(sc, "v_rm")
                        p.wait(sc, "v_acc", p.lo("v_acc", 1))
                        p.tag(sc.activation(out=RS, in_=DM, func=AF.Exp), "s_resc")
                        p.wait(sc, "pe_t", p.lo("pe_t", 1))
                        p.tag(sc.activation(out=PP, in_=PS, func=AF.Exp,
                                            bias=sb["at_nm"][:, nxt:nxt + 1]), "s_pexp")

                    @blk.tensor
                    def _(te, jj=jj, PP=PP, PTp=PTp):
                        p.wait(te, "s_pexp")
                        p.wait(te, "s_ptev", p.lo("s_ptev", 1))
                        for ss in range(8):
                            ii = te.transpose(PTp[:, ss * 128:(ss + 1) * 128],
                                              PP[:, ss * 128:(ss + 1) * 128], sb["id_b"][:])
                        p.tag(ii, "pe_t")

                    @blk.vector
                    def _(v, jj=jj, j=j, pp=pp, PT=PT, PTp=PTp):
                        p.wait(v, "pe_t")
                        p.wait(v, "pe_av", p.lo("pe_av", 1))
                        p.tag(v.tensor_tensor(out=PT, in0=PTp,
                                              in1=sb["at_mk"][:, pp * 8 * BLK + j * BLK:
                                                              pp * 8 * BLK + (j + 1) * BLK],
                                              op=ALU.mult), "s_ptev")

                    @blk.tensor
                    def _(te, jj=jj, j=j, PT=PT):
                        p.wait(te, "s_ptev")
                        p.wait(te, "v_acc")
                        for ss in range(8):
                            kt = j * 8 + ss
                            ii = te.matmul(sb["at_pav"][:], PT[:, ss * 128:(ss + 1) * 128],
                                           sb["at_v"][:, kt * 129:(kt + 1) * 129],
                                           start=(ss == 0), stop=(ss == 7))
                        p.tag(ii, "pe_av")

                    @blk.vector
                    def _(v, jj=jj, RS=RS):
                        p.wait(v, "pe_av")
                        p.wait(v, "s_resc")
                        p.tag(v.scalar_tensor_tensor(out=sb["at_acc"][:], in0=sb["at_acc"][:],
                                                     scalar=RS, in1=sb["at_pav"][:],
                                                     op0=ALU.mult, op1=ALU.add), "v_acc")

                @blk.vector
                def _(v, hh=hh):
                    p.wait(v, "v_acc")
                    p.chain(v, v.reciprocal(out=sb["at_rf"][:], in_=sb["at_acc"][:, 128:129]))
                    p.tag(v.tensor_scalar(out=sb["at_att"][:], in0=sb["at_acc"][:, 0:128],
                                          scalar1=sb["at_rf"][:], scalar2=None, op0=ALU.mult), "v_att")

                @blk.tensor
                def _(te, hh=hh):
                    p.wait(te, "v_att")
                    p.wait(te, "v_x1")
                    p.tag(te.transpose(sb["at_ptr"][:], sb["at_att"][:], sb["id_f"][:]), "pe_tr")

                @blk.sync
                def _(s, blkslot=blkslot, i=i, g=g, hh=hh):
                    p.wait(s, "v_x1")
                    h = 4 * g + hh
                    loc = blkslot * BLK + i * 128
                    p.dma(s, "q_xo", sb["at_xo"][:], dt["xTown"][h * 128:(h + 1) * 128, loc:loc + 128])

                @blk.vector
                def _(v):
                    p.wait(v, "pe_tr")
                    p.wait(v, "q_xo")
                    p.wait(v, "q_x1")
                    p.tag(v.tensor_tensor(out=sb["at_x1"][:], in0=sb["at_ptr"][:],
                                          in1=sb["at_xo"][:], op=ALU.add), "v_x1")

                @blk.sync
                def _(s, blkslot=blkslot, i=i, g=g, hh=hh):
                    p.wait(s, "v_x1")
                    h = 4 * g + hh
                    loc = blkslot * BLK + i * 128
                    p.dma(s, "q_x1", dt["x1T_s"][h * 128:(h + 1) * 128, loc:loc + 128], sb["at_x1"][:])

            snaps[gp] = (p.cnt("s_ptev"), p.cnt("pe_s"))



def _dequant_one(p, dt, sb, pk_ap, sc_ap, out_s, jh, rows, oc0, ocw, n_groups):
    blk = p.block
    NJ = {2048: 256, 5632: 704}[out_s.shape[0]]

    @blk.sync
    def _(s):
        p.wait(s, "dq_done")
        p.dma(s, "q_dq", sb["dq_pk"][0:rows, 0:ocw], pk_ap)
        p.dma(s, "q_dq", sb["dq_scf"][0:rows, 0:ocw], sc_ap)

    @blk.scalar
    def _(sc):
        p.wait(sc, "q_dq")
        p.wait(sc, "s_dqw")
        p.tag(sc.activation(out=sb["dq_sc"][0:rows, 0:ocw], in_=sb["dq_scf"][0:rows, 0:ocw],
                            func=AF.Copy), "s_dqsc")

    for k in range(8):
        eng_is_v = (k % 2 == 0)

        @blk.vector
        def _(v, k=k):
            p.wait(v, "q_dq")
            p.wait(v, "q_dqs")
            X = sb["dq_n"][0:rows, 0:ocw]
            T = sb["dq_t"][0:rows, 0:ocw]
            Sg = sb["dq_sg"][0:rows, 0:ocw]
            R1 = sb["dq_r1"][0:rows, 0:ocw]
            R2 = sb["dq_r2"][0:rows, 0:ocw]
            VF = sb["dq_vf"][0:rows, 0:ocw]
            p.chain(v, v.tensor_scalar(out=X, in0=sb["dq_pk"][0:rows, 0:ocw], scalar1=4 * k, scalar2=15,
                                       op0=ALU.logical_shift_right, op1=ALU.bitwise_and))
            p.chain(v, v.tensor_scalar(out=T, in0=X, scalar1=7, scalar2=None, op0=ALU.bitwise_and))
            p.chain(v, v.tensor_scalar(out=Sg, in0=X, scalar1=3, scalar2=None,
                                       op0=ALU.logical_shift_right))
            p.chain(v, v.tensor_copy(out=sb["dq_sgf"][0:rows, 0:ocw], in_=Sg))
            p.chain(v, v.tensor_scalar(out=sb["dq_sgf"][0:rows, 0:ocw], in0=sb["dq_sgf"][0:rows, 0:ocw],
                                       scalar1=-2.0, scalar2=1.0, op0=ALU.mult, op1=ALU.add))
            p.chain(v, v.tensor_copy(out=VF, in_=T))
            p.chain(v, v.tensor_scalar(out=R1, in0=VF, scalar1=-4.0, scalar2=0.0,
                                       op0=ALU.add, op1=ALU.max))
            p.chain(v, v.tensor_scalar(out=R2, in0=VF, scalar1=-6.0, scalar2=0.0,
                                       op0=ALU.add, op1=ALU.max))
            p.chain(v, v.tensor_tensor(out=VF, in0=VF, in1=R1, op=ALU.add))
            p.chain(v, v.scalar_tensor_tensor(out=VF, in0=VF, scalar=0.5, in1=R2,
                                              op0=ALU.mult, op1=ALU.add))
            p.chain(v, v.tensor_tensor(out=VF, in0=VF, in1=sb["dq_sgf"][0:rows, 0:ocw], op=ALU.mult))
            p.wait(v, "s_dqsc")
            p.tag(v.tensor_tensor(out=sb["dq_w"][0:rows, 0:ocw], in0=VF,
                                  in1=sb["dq_sc"][0:rows, 0:ocw], op=ALU.mult), "s_dqw")

        @blk.sync
        def _(s, k=k):
            p.wait(s, "s_dqw")
            p.dma(s, "q_dqs", out_s[k * NJ + jh * 128: k * NJ + jh * 128 + rows, oc0:oc0 + ocw],
                  sb["dq_w"][0:rows, 0:ocw])

    @blk.gpsimd
    def _(g):
        p.wait(g, "q_dqs")
        p.tag(g.memset(sb["dq_nop"][0:1, 0:1], 0.0), "dq_done")


def ph_dequant(p, dt, sb, S_, P_):
    W = 2048
    S_("dq_pk", [128, W], I32)
    S_("dq_scf", [128, W], F16)
    S_("dq_sc", [128, W], F32)
    S_("dq_n", [128, W], I32)
    S_("dq_t", [128, W], I32)
    S_("dq_sg", [128, W], I32)
    S_("dq_sgf", [128, W], F32)
    S_("dq_r1", [128, W], F32)
    S_("dq_r2", [128, W], F32)
    S_("dq_vf", [128, W], F32)
    S_("dq_w", [128, W], BF16)
    S_("dq_nop", [1, 1], F32)
    ocs = [(i * W, min(W, OGU - i * W)) for i in range((OGU + W - 1) // W)]
    for jh in range(2):
        for (oc0, ocw) in ocs:
            grow = 8 * jh
            _dequant_one(p, dt, sb,
                         dt["gupT"][jh * 128:(jh + 1) * 128, oc0:oc0 + ocw],
                         dt["gus"][grow:grow + 8, None, oc0:oc0 + ocw].broadcast_to([8, 16, ocw]),
                         dt["wgu_s"], jh, 128, oc0, ocw, 16)
    for jh in range(6):
        rows = 128 if jh < 5 else 64
        grow = 8 * jh
        ng = 8 if jh < 5 else 4
        _dequant_one(p, dt, sb,
                     dt["dnpT"][jh * 128:jh * 128 + rows, :],
                     dt["dns"][grow:grow + ng, None, :].broadcast_to([ng, 16, E]),
                     dt["wdn_s"], jh, rows, 0, E, 44)



def _ffn_ln(p, dt, sb, src, t, TC2):
    blk = p.block
    HF = FC // 2
    for half in range(2):
        @blk.sync
        def _(s, half=half):
            p.wait(s, "f_vn")
            p.wait(s, "f_vn3")
            p.wait(s, "f_pestat")
            p.dma(s, "q_fx", sb["f_x32"][:].rearrange("p (f s) -> p f s", f=HF),
                  dt[src].rearrange("(f p) s -> p f s", p=128)[:, half * HF:(half + 1) * HF,
                                                              t * TC2:(t + 1) * TC2])

        @blk.scalar
        def _(sc, half=half):
            p.wait(sc, "q_fx")
            p.wait(sc, "f_pestat")
            p.wait(sc, "q_fh")
            p.wait(sc, "q_fh2")
            for f in range(HF):
                p.tag(sc.activation(out=sb["f_xh"][:, (half * HF + f) * TC2:(half * HF + f + 1) * TC2],
                                    in_=sb["f_x32"][:, f * TC2:(f + 1) * TC2], func=AF.Copy), "f_cast")
                p.tag(sc.activation(out=sb["f_xst"][:, f * TC2:(f + 1) * TC2],
                                    in_=sb["f_x32"][:, f * TC2:(f + 1) * TC2], func=AF.Square), "f_cast")

        @blk.tensor
        def _(te, half=half):
            p.wait(te, "f_cast")
            p.wait(te, "v_init")
            if half == 0:
                p.wait(te, "f_vstat")
            for f in range(HF):
                te.matmul(sb["f_ps"][0:1, 0:TC2], sb["ones_b"][:],
                          sb["f_xh"][:, (half * HF + f) * TC2:(half * HF + f + 1) * TC2],
                          start=(half == 0 and f == 0), stop=(half == 1 and f == HF - 1))
            for f in range(HF):
                i = te.matmul(sb["f_ps"][0:1, TC2:2 * TC2], sb["ones_b"][:],
                              sb["f_xst"][:, f * TC2:(f + 1) * TC2],
                              start=(half == 0 and f == 0), stop=(half == 1 and f == HF - 1))
            p.tag(i, "f_pestat")

    @blk.vector
    def _(v):
        p.wait(v, "f_pestat")
        p.chain(v, v.tensor_scalar(out=sb["f_st"][0:1, 0:TC2], in0=sb["f_ps"][0:1, 0:TC2],
                                   scalar1=1.0 / E, scalar2=None, op0=ALU.mult))
        v.tensor_scalar(out=sb["f_st"][0:1, TC2:2 * TC2], in0=sb["f_ps"][0:1, TC2:2 * TC2],
                        scalar1=1.0 / E, scalar2=LN_EPS, op0=ALU.mult, op1=ALU.add)
        p.chain(v, v.tensor_tensor(out=sb["f_ps"][0:1, 0:TC2], in0=sb["f_st"][0:1, 0:TC2],
                                   in1=sb["f_st"][0:1, 0:TC2], op=ALU.mult))
        p.tag(v.tensor_tensor(out=sb["f_st"][0:1, TC2:2 * TC2], in0=sb["f_st"][0:1, TC2:2 * TC2],
                              in1=sb["f_ps"][0:1, 0:TC2], op=ALU.subtract), "f_vstat")

    @blk.scalar
    def _(sc):
        p.wait(sc, "f_vstat")
        p.chain(sc, sc.activation(out=sb["f_st"][0:1, TC2:2 * TC2], in_=sb["f_st"][0:1, TC2:2 * TC2],
                                  func=AF.Ln))
        p.tag(sc.activation(out=sb["f_st"][0:1, TC2:2 * TC2], in_=sb["f_st"][0:1, TC2:2 * TC2],
                            func=AF.Exp, scale=-0.5), "f_rstd")

    @blk.tensor
    def _(te):
        p.wait(te, "f_rstd")
        p.wait(te, "f_vn")
        te.matmul(sb["f_pb"][:, 0:TC2], sb["ones_row"][:], sb["f_st"][0:1, 0:TC2], start=True, stop=True)
        p.tag(te.matmul(sb["f_pb"][:, TC2:2 * TC2], sb["ones_row"][:], sb["f_st"][0:1, TC2:2 * TC2],
                        start=True, stop=True), "f_pebc")

    for half in range(2):
        @blk.sync
        def _(s, half=half):
            p.wait(s, "f_vn")
            if half == 0:
                p.wait(s, "f_pestat")
            else:
                p.wait(s, "f_vn2")
            p.dma(s, "q_fx2", sb["f_x32"][:].rearrange("p (f s) -> p f s", f=HF),
                  dt[src].rearrange("(f p) s -> p f s", p=128)[:, half * HF:(half + 1) * HF,
                                                              t * TC2:(t + 1) * TC2])

        @blk.vector
        def _(v, half=half):
            p.wait(v, "q_fx2")
            p.wait(v, "f_pebc")
            for f in range(HF):
                i = v.tensor_tensor(out=sb["f_x32"][:, f * TC2:(f + 1) * TC2],
                                    in0=sb["f_x32"][:, f * TC2:(f + 1) * TC2],
                                    in1=sb["f_pb"][:, 0:TC2], op=ALU.subtract)
            p.chain(v, i)
            for f in range(HF):
                i = v.tensor_tensor(out=sb["f_xh"][:, (half * HF + f) * TC2:(half * HF + f + 1) * TC2],
                                    in0=sb["f_x32"][:, f * TC2:(f + 1) * TC2],
                                    in1=sb["f_pb"][:, TC2:2 * TC2], op=ALU.mult)
            p.tag(i, "f_vn2" if half == 0 else "f_vn")


def _ffn(p, dt, sb, src, dst, final=False):
    blk = p.block
    TC2 = 512
    NW = 6
    for t in range(TOK // TC2):
        _ffn_ln(p, dt, sb, src, t, TC2)

        @blk.sync
        def _(s, t=t):
            p.wait(s, "f_vn")
            p.dma(s, "q_fh", dt["xh_s"].rearrange("(f p) s -> p f s", p=128)[:, :, t * TC2:(t + 1) * TC2],
                  sb["f_xh"][:].rearrange("p (f s) -> p f s", f=FC))

        if FFN_STAGE < 2:
            continue

        @blk.sync
        def _(s, t=t):
            p.wait(s, "q_fh")
            p.wait(s, "f_peg")
            for k in range(8):
                for jh in range(2):
                    c = k * 2 + jh
                    st0 = 8 * 128 * jh + k
                    p.dma(s, "q_fh2", sb["f_xh"][:, c * TC2:(c + 1) * TC2],
                          dt["xh_s"][st0: st0 + 8 * 127 + 1:8, t * TC2:(t + 1) * TC2])

        for w in range(NW):
            ow = 1024 if w < 5 else 512
            nsl = ow // 8

            @blk.gpsimd
            def _(s, w=w, ow=ow):
                p.wait(s, "f_peg")
                p.dma(s, "q_fw", sb["f_wg"][:, 0:FC * ow].rearrange("p (i o) -> p i o", i=FC),
                      dt["wgu_s"].rearrange("(i p) o -> p i o", p=128)[:, :, 1024 * w:1024 * w + ow])
                p.dma(s, "q_fw", sb["f_wu"][:, 0:FC * ow].rearrange("p (i o) -> p i o", i=FC),
                      dt["wgu_s"].rearrange("(i p) o -> p i o", p=128)[:, :, FI + 1024 * w:FI + 1024 * w + ow])

            for k in range(8):
                slot = k * 6 + w
                kp = k % 2
                PG = sb["f_pg0"] if kp == 0 else sb["f_pg1"]
                PU = sb["f_pu0"] if kp == 0 else sb["f_pu1"]
                SG = sb["f_sg"][:, kp * TC2:(kp + 1) * TC2]

                @blk.tensor
                def _(te, k=k, w=w, ow=ow, nsl=nsl, PG=PG, PU=PU):
                    p.wait(te, "q_fw")
                    p.wait(te, "q_fh2")
                    p.wait(te, "f_vstat")
                    p.wait(te, "s_sil", p.lo("s_sil", 1))
                    p.wait(te, "v_h", p.lo("v_h", 1))
                    p.wait(te, "f_ped")
                    for f in range(FC):
                        te.matmul(PG[0:nsl, :], sb["f_wg"][:, f * ow + k: f * ow + k + 8 * (nsl - 1) + 1:8],
                                  sb["f_xh"][:, f * TC2:(f + 1) * TC2], start=(f == 0), stop=(f == FC - 1))
                    for f in range(FC):
                        i = te.matmul(PU[0:nsl, :], sb["f_wu"][:, f * ow + k: f * ow + k + 8 * (nsl - 1) + 1:8],
                                      sb["f_xh"][:, f * TC2:(f + 1) * TC2], start=(f == 0), stop=(f == FC - 1))
                    p.tag(i, "f_peg")

                @blk.scalar
                def _(sc, nsl=nsl, PG=PG, SG=SG):
                    p.wait(sc, "f_peg")
                    p.wait(sc, "v_h", p.lo("v_h", 1))
                    p.tag(sc.activation(out=SG[0:nsl, :], in_=PG[0:nsl, :],
                                        func=AF.Silu), "s_sil")

                @blk.vector
                def _(v, slot=slot, nsl=nsl, PU=PU, SG=SG):
                    p.wait(v, "s_sil")
                    p.wait(v, "f_peg")
                    p.tag(v.tensor_tensor(out=sb["f_h"][0:nsl, slot * TC2:(slot + 1) * TC2],
                                          in0=SG[0:nsl, :], in1=PU[0:nsl, :],
                                          op=ALU.mult), "v_h")

        if FFN_STAGE < 3:
            continue
        for og in range(2):
            first = True
            for k in range(8):
                for w in range(6):
                    nsl = 128 if w < 5 else 64
                    slot = k * 6 + w
                    wrow = k * 704 + 128 * w
                    last = (k == 7 and w == 5)

                    ci = k * 6 + w
                    wpar = ci % 2

                    @blk.gpsimd
                    def _(s, og=og, nsl=nsl, wrow=wrow, wpar=wpar):
                        p.wait(s, "f_ped", p.lo("f_ped", 1))
                        if og == 0 and wrow == 0:
                            p.wait(s, "v_h")
                        p.dma(s, "q_fd0" if wpar == 0 else "q_fd1",
                              sb["f_wd2"][0:nsl, wpar * 1024:(wpar + 1) * 1024],
                              dt["wdn_s"][wrow:wrow + nsl, og * 1024:(og + 1) * 1024])

                    @blk.tensor
                    def _(te, og=og, nsl=nsl, slot=slot, first=first, last=last, wpar=wpar):
                        p.wait(te, "q_fd0" if wpar == 0 else "q_fd1")
                        p.wait(te, "v_h")
                        if first:
                            p.wait(te, "v_y")
                            p.wait(te, "f_pebc")
                            p.wait(te, "f_vn")
                        for o2 in range(8):
                            i = te.matmul(sb["f_pd"][:, o2 * TC2:(o2 + 1) * TC2],
                                          sb["f_wd2"][0:nsl, wpar * 1024 + o2 * 128: wpar * 1024 + (o2 + 1) * 128],
                                          sb["f_h"][0:nsl, slot * TC2:(slot + 1) * TC2],
                                          start=first, stop=last)
                        p.tag(i, "f_ped")
                    first = False

            @blk.vector
            def _(v, og=og):
                p.wait(v, "f_ped")
                p.wait(v, "q_fo")
                for o2 in range(8):
                    i = v.tensor_copy(out=sb["f_y"][:, (og * 8 + o2) * TC2:(og * 8 + o2 + 1) * TC2],
                                      in_=sb["f_pd"][:, o2 * TC2:(o2 + 1) * TC2])
                p.tag(i, "v_y")

        if FFN_STAGE < 4:
            continue
        @blk.scalar
        def _(sc):
            p.wait(sc, "v_y")
            p.wait(sc, "f_pestat")
            for f in range(FC):
                p.tag(sc.activation(out=sb["f_xh"][:, f * TC2:(f + 1) * TC2],
                                    in_=sb["f_y"][:, f * TC2:(f + 1) * TC2], func=AF.Square), "f_ysq")

        @blk.tensor
        def _(te):
            p.wait(te, "f_ysq")
            p.wait(te, "f_vstat")
            p.wait(te, "v_y")
            for f in range(FC):
                i = te.matmul(sb["f_ps"][0:1, 0:TC2], sb["ones_b"][:], sb["f_xh"][:, f * TC2:(f + 1) * TC2],
                              start=(f == 0), stop=(f == FC - 1))
            p.tag(i, "f_pestat")

        @blk.vector
        def _(v):
            p.wait(v, "f_pestat")
            p.tag(v.tensor_scalar(out=sb["f_st"][0:1, 0:TC2], in0=sb["f_ps"][0:1, 0:TC2],
                                  scalar1=1.0 / E, scalar2=RMS_EPS, op0=ALU.mult, op1=ALU.add), "f_vstat")

        @blk.scalar
        def _(sc):
            p.wait(sc, "f_vstat")
            p.chain(sc, sc.activation(out=sb["f_st"][0:1, 0:TC2], in_=sb["f_st"][0:1, 0:TC2], func=AF.Ln))
            p.tag(sc.activation(out=sb["f_st"][0:1, 0:TC2], in_=sb["f_st"][0:1, 0:TC2],
                                func=AF.Exp, scale=-0.5), "f_rstd2")

        @blk.tensor
        def _(te):
            p.wait(te, "f_rstd2")
            p.wait(te, "f_vn3")
            p.tag(te.matmul(sb["f_pb"][:, 0:TC2], sb["ones_row"][:], sb["f_st"][0:1, 0:TC2],
                            start=True, stop=True), "f_pebc2")

        HF = FC // 2
        for half in range(2):
            @blk.sync
            def _(s, half=half, t=t):
                p.wait(s, "f_vn3")
                p.wait(s, "f_pestat")
                p.dma(s, "q_fr", sb["f_x32"][:].rearrange("p (f s) -> p f s", f=HF),
                      dt[src].rearrange("(f p) s -> p f s", p=128)[:, half * HF:(half + 1) * HF,
                                                                  t * TC2:(t + 1) * TC2])

            @blk.vector
            def _(v, half=half):
                p.wait(v, "q_fr")
                p.wait(v, "f_pebc2")
                for f in range(HF):
                    ff = half * HF + f
                    i = v.tensor_tensor(out=sb["f_y"][:, ff * TC2:(ff + 1) * TC2],
                                        in0=sb["f_y"][:, ff * TC2:(ff + 1) * TC2],
                                        in1=sb["f_pb"][:, 0:TC2], op=ALU.mult)
                p.chain(v, i)
                for f in range(HF):
                    ff = half * HF + f
                    i = v.tensor_tensor(out=sb["f_y"][:, ff * TC2:(ff + 1) * TC2],
                                        in0=sb["f_y"][:, ff * TC2:(ff + 1) * TC2],
                                        in1=sb["f_x32"][:, f * TC2:(f + 1) * TC2], op=ALU.add)
                p.tag(i, "f_vn3")

            @blk.sync
            def _(s, half=half, t=t):
                p.wait(s, "f_vn3")
                p.dma(s, "q_fo", dt[dst].rearrange("(f p) s -> p f s", p=128)[:, half * HF:(half + 1) * HF,
                                                                              t * TC2:(t + 1) * TC2],
                      sb["f_y"][:, half * HF * TC2:(half + 1) * HF * TC2].rearrange("p (f s) -> p f s", f=HF))


def _ffn_alloc(p, dt, sb, S_, P_):
    TC2 = 512
    S_("f_x32", [128, 8 * TC2], F32)
    S_("f_xst", [128, 8 * TC2], BF16)
    S_("f_xh", [128, FC * TC2], BF16)
    S_("f_st", [1, 2 * TC2], F32)
    S_("f_wg", [128, FC * 1024], BF16)
    S_("f_wu", [128, FC * 1024], BF16)
    S_("f_wd2", [128, 2 * 1024], BF16)
    S_("f_h", [128, 48 * TC2], BF16)
    S_("f_y", [128, FC * TC2], F32)
    S_("f_sg", [128, 2 * TC2], BF16)
    pool = P_("f_pool", [128, 8 * TC2])
    sb["f_pd"] = pool[:]
    sb["f_pb"] = pool[:, 0:1024]
    sb["f_pg0"] = pool[:, 2048:2560]
    sb["f_pu0"] = pool[:, 2560:3072]
    sb["f_pg1"] = pool[:, 3072:3584]
    sb["f_pu1"] = pool[:, 3584:4096]
    sb["f_ps"] = pool[:, 1024:2048]


def ph_ffn1(p, dt, sb, S_, P_):
    _ffn_alloc(p, dt, sb, S_, P_)
    _ffn(p, dt, sb, "x1T_s", "x2T_s")


def ph_ffn2(p, dt, sb, S_, P_):
    _ffn_alloc(p, dt, sb, S_, P_)
    _ffn(p, dt, sb, "x3T_s", "out", final=True)
    dt["__out_written__"] = True




def ph_mixfront(p, dt, sb, S_, P_):
    blk = p.block
    TC2 = 512
    S_("f_x32", [128, 8 * TC2], F32)
    S_("f_xst", [128, 8 * TC2], BF16)
    S_("f_xh", [128, FC * TC2], BF16)
    S_("f_st", [1, 2 * TC2], F32)
    mxpool = P_("mx_pool", [128, 4096])
    sb["f_ps"] = mxpool[:, 3072:4096]
    sb["f_pb"] = mxpool[:, 0:1024]

    for t in range(TOK // TC2):
        _ffn_ln(p, dt, sb, "x2T_s", t, TC2)

        @blk.sync
        def _(s, t=t):
            p.wait(s, "f_vn")
            p.dma(s, "q_fh", dt["xh_s"].rearrange("(f p) s -> p f s", p=128)[:, :, t * TC2:(t + 1) * TC2],
                  sb["f_xh"][:].rearrange("p (f s) -> p f s", f=FC))

    S_("mx_halo", [128, 2 * 48], BF16)
    S_("mx_tails", [128, 16 * 48], BF16)
    S_("mx_hA", [128, 48], BF16)
    S_("mx_hB", [128, 48], BF16)
    S_("mx_sel", [128, 4 * 16], F32)

    @blk.sync
    def _(s):
        p.wait(s, "q_fh")
        for bs in range(2):
            p.dma(s, "q_hl", dt["halo_in"][:, bs * 48:(bs + 1) * 48].rearrange("p (f c) -> p f c", f=FC),
                  dt["xh_s"].rearrange("(f p) s -> p f s", p=128)[:, :, (bs + 1) * BLK - 3:(bs + 1) * BLK])
        p.dma(s, "q_hl", sb["mx_sel"][:, 0:16], dt["halA"].partition_broadcast(128))
        p.dma(s, "q_hl", sb["mx_sel"][:, 16:32], dt["halB"].partition_broadcast(128))
        p.dma(s, "q_hl", sb["mx_sel"][:, 32:48], dt["selA"].partition_broadcast(128))
        p.dma(s, "q_hl", sb["mx_sel"][:, 48:64], dt["selB"].partition_broadcast(128))

    @blk.gpsimd
    def _(g):
        p.wait(g, "q_hl")
        i = g.collective_compute("AllGather", ALU.bypass,
                                 replica_groups=[list(range(8))],
                                 ins=[dt["halo_in"].opt()], outs=[dt["halo_out"].opt()])
        p.tag(i, "cc_halo")

    @blk.sync
    def _(s):
        p.wait(s, "cc_halo")
        for idx in range(16):
            b_, pos = idx // 8, idx % 8
            r = (pos if pos < 4 else 7 - pos) + 4 * b_
            slot = 0 if pos < 4 else 1
            p.dma(s, "q_hl2", sb["mx_tails"][:, idx * 48:(idx + 1) * 48],
                  dt["halo_out"][r * 128:(r + 1) * 128, slot * 48:(slot + 1) * 48])

    @blk.vector
    def _(v):
        p.wait(v, "q_hl2")
        v.memset(sb["mx_hA"][:], 0.0)
        p.chain(v, v.memset(sb["mx_hB"][:], 0.0))
        for idx in range(16):
            p.chain(v, v.scalar_tensor_tensor(out=sb["mx_hA"][:], in0=sb["mx_tails"][:, idx * 48:(idx + 1) * 48],
                                              scalar=sb["mx_sel"][:, idx:idx + 1], in1=sb["mx_hA"][:],
                                              op0=ALU.mult, op1=ALU.add))
        for idx in range(16):
            p.chain(v, v.scalar_tensor_tensor(out=sb["mx_hB"][:], in0=sb["mx_tails"][:, idx * 48:(idx + 1) * 48],
                                              scalar=sb["mx_sel"][:, 16 + idx:17 + idx], in1=sb["mx_hB"][:],
                                              op0=ALU.mult, op1=ALU.add))
        p.tag(v.nop(), "v_halo")

    S_("mx_xh", [128, FC * TOK], BF16)
    S_("mx_w32", [128, FC * 256], F32)
    S_("mx_wb", [128, 2 * FC * 256], BF16)
    S_("mx_cin", [128, BLK + 3], F32)
    S_("mx_cacc", [128, BLK], F32)
    S_("mx_cw", [128, 32 * KC], F32)
    S_("mx_cb", [128, 32], F32)
    S_("mx_ev", [128, BLK], BF16)
    S_("mx_gt", [128, BLK], BF16)
    sb["mx_px"] = sb["mx_pool"][:, 0:1024]
    sb["mx_pg"] = sb["mx_pool"][:, 1024:2048]
    sb["mx_ph"] = sb["mx_pool"][:, 2048:2056]

    @blk.sync
    def _(s):
        p.wait(s, "q_fh")
        p.wait(s, "f_pestat")
        p.dma(s, "q_mx", sb["mx_xh"][:].rearrange("p (f s) -> p f s", f=FC),
              dt["xh_s"].rearrange("(f p) s -> p f s", p=128))
        p.dma(s, "q_mx", sb["mx_cw"][:].rearrange("p (c k) -> p c k", k=KC),
              dt["convw"].rearrange("(c p) k -> p c k", p=128))
        p.dma(s, "q_mx", sb["mx_cb"][:], dt["convb"])

    for oc in range(32):
        wp = oc % 2
        W32 = sb["mx_w32"][:]
        WB = sb["mx_wb"][:, wp * FC * 256:(wp + 1) * FC * 256]

        @blk.gpsimd
        def _(s, oc=oc, W32=W32):
            p.wait(s, "c_mxw")
            p.dma(s, "q_mw", W32.rearrange("p (f d) -> p f d", f=FC),
                  dt["inT"].rearrange("(f p) d -> p f d", p=128)[:, :, oc * 256:(oc + 1) * 256])

        @blk.scalar
        def _(sc, oc=oc, W32=W32, WB=WB):
            p.wait(sc, "q_mw")
            p.wait(sc, "pe_mx", p.lo("pe_mx", 2))
            p.tag(sc.activation(out=WB, in_=W32, func=AF.Copy), "c_mxw")

        for bs in range(2):
            @blk.tensor
            def _(te, oc=oc, bs=bs, WB=WB):
                p.wait(te, "c_mxw")
                p.wait(te, "q_mx")
                p.wait(te, "v_halo")
                p.wait(te, "s_cv")
                p.wait(te, "s_gt")
                for f in range(FC):
                    te.matmul(sb["mx_px"][:, 0:512], WB[:, f * 256: f * 256 + 255:2],
                              sb["mx_xh"][:, f * TOK + bs * BLK: f * TOK + bs * BLK + 512],
                              start=(f == 0), stop=(f == FC - 1))
                    te.matmul(sb["mx_px"][:, 512:1024], WB[:, f * 256: f * 256 + 255:2],
                              sb["mx_xh"][:, f * TOK + bs * BLK + 512: f * TOK + (bs + 1) * BLK],
                              start=(f == 0), stop=(f == FC - 1))
                    te.matmul(sb["mx_ph"][:, 0:3], WB[:, f * 256: f * 256 + 255:2],
                              (sb["mx_hA"] if bs == 0 else sb["mx_hB"])[:, f * 3:(f + 1) * 3],
                              start=(f == 0), stop=(f == FC - 1))
                    te.matmul(sb["mx_pg"][:, 0:512], WB[:, f * 256 + 1: f * 256 + 256:2],
                              sb["mx_xh"][:, f * TOK + bs * BLK: f * TOK + bs * BLK + 512],
                              start=(f == 0), stop=(f == FC - 1))
                    i = te.matmul(sb["mx_pg"][:, 512:1024], WB[:, f * 256 + 1: f * 256 + 256:2],
                                  sb["mx_xh"][:, f * TOK + bs * BLK + 512: f * TOK + (bs + 1) * BLK],
                                  start=(f == 0), stop=(f == FC - 1))
                p.tag(i, "pe_mx")

            @blk.scalar
            def _(sc, oc=oc, bs=bs):
                p.wait(sc, "pe_mx")
                p.wait(sc, "v_cv")
                p.chain(sc, sc.activation(out=sb["mx_cin"][:, 0:3], in_=sb["mx_ph"][:, 0:3], func=AF.Copy))
                p.tag(sc.activation(out=sb["mx_cin"][:, 3:3 + BLK], in_=sb["mx_px"][:], func=AF.Copy), "s_cv")

            @blk.vector
            def _(v, oc=oc, bs=bs):
                p.wait(v, "s_cv")
                p.wait(v, "s_cvx")
                p.chain(v, v.tensor_scalar(out=sb["mx_cacc"][:], in0=sb["mx_cin"][:, 0:BLK],
                                           scalar1=sb["mx_cw"][:, oc * KC:oc * KC + 1], scalar2=None,
                                           op0=ALU.mult))
                for k2 in range(1, KC):
                    p.chain(v, v.scalar_tensor_tensor(out=sb["mx_cacc"][:], in0=sb["mx_cin"][:, k2:k2 + BLK],
                                                      scalar=sb["mx_cw"][:, oc * KC + k2:oc * KC + k2 + 1],
                                                      in1=sb["mx_cacc"][:], op0=ALU.mult, op1=ALU.add))
                p.tag(v.nop(), "v_cv")

            @blk.scalar
            def _(sc, oc=oc, bs=bs):
                p.wait(sc, "v_cv")
                p.wait(sc, "q_cvx")
                p.tag(sc.activation(out=sb["mx_ev"][:], in_=sb["mx_cacc"][:], func=AF.Silu,
                                    bias=sb["mx_cb"][:, oc:oc + 1]), "s_cvx")

            @blk.sync
            def _(s, oc=oc, bs=bs):
                p.wait(s, "s_cvx")
                p.dma(s, "q_cvx", dt["convx_s"][oc * 128:(oc + 1) * 128, bs * BLK:(bs + 1) * BLK],
                      sb["mx_ev"][:])

            @blk.scalar
            def _(sc, oc=oc, bs=bs):
                p.wait(sc, "pe_mx")
                p.wait(sc, "q_gt")
                p.tag(sc.activation(out=sb["mx_gt"][:], in_=sb["mx_pg"][:], func=AF.Silu), "s_gt")

            @blk.sync
            def _(s, oc=oc, bs=bs):
                p.wait(s, "s_gt")
                p.dma(s, "q_gt", dt["gate_s"][oc * 128:(oc + 1) * 128, bs * BLK:(bs + 1) * BLK],
                      sb["mx_gt"][:])

    S_("mx_xp32", [128, FC * 160], F32)
    S_("mx_xpb", [128, FC * 160], BF16)
    S_("mx_dtT32", [128, FC * 128], F32)
    S_("mx_dtT", [128, FC * 128], BF16)
    S_("mx_cx", [128, FC * TC2], BF16)
    S_("mx_dtin", [128, TC2], BF16)
    S_("mx_bc", [32, TC2], BF16)
    S_("mx_dtb", [128, 32], F32)
    S_("mx_sp1", [128, TC2], F32)
    S_("mx_sp2", [128, TC2], F32)
    S_("mx_dtev", [128, TC2], BF16)
    sb["mx_pdtin"] = sb["mx_pool"][:, 0:512]
    sb["mx_pbc"] = sb["mx_pool"][:, 1024:1536]
    sb["mx_pdt"] = sb["mx_pool"][:, 2048:2560]

    @blk.sync
    def _(s):
        p.dma(s, "q_xp", sb["mx_dtb"][:], dt["dtb"])

    for h in range(NH):
        @blk.sync
        def _(s, h=h):
            p.wait(s, "c_xpw")
            p.wait(s, "pe_sp")
            p.dma(s, "q_xp", sb["mx_xp32"][:].rearrange("p (f r) -> p f r", f=FC),
                  dt["xpT"][h].rearrange("(f p) r -> p f r", p=128))
            p.dma(s, "q_xp", sb["mx_dtT32"][:], dt["dtT"][h])

        @blk.scalar
        def _(sc, h=h):
            p.wait(sc, "q_xp")
            p.tag(sc.activation(out=sb["mx_xpb"][:], in_=sb["mx_xp32"][:], func=AF.Copy), "c_xpw")
            p.tag(sc.activation(out=sb["mx_dtT"][:], in_=sb["mx_dtT32"][:], func=AF.Copy), "c_xpw")

        for bs in range(2):
            for t2 in range(2):
                col0 = bs * BLK + t2 * TC2

                @blk.sync
                def _(s, h=h, col0=col0):
                    p.wait(s, "pe_sp")
                    p.wait(s, "pe_dt")
                    p.wait(s, "q_cvx")
                    p.dma(s, "q_cx", sb["mx_cx"][:].rearrange("p (f s) -> p f s", f=FC),
                          dt["convx_s"].rearrange("(f p) s -> p f s", p=128)[:, h * FC:(h + 1) * FC,
                                                                             col0:col0 + TC2])

                @blk.tensor
                def _(te, h=h, col0=col0):
                    p.wait(te, "q_cx")
                    p.wait(te, "c_xpw")
                    p.wait(te, "s_spev")
                    p.wait(te, "v_bcev")
                    for f in range(FC):
                        te.matmul(sb["mx_pdtin"], sb["mx_xpb"][:, f * 160:f * 160 + 128],
                                  sb["mx_cx"][:, f * TC2:(f + 1) * TC2], start=(f == 0), stop=(f == FC - 1))
                    for f in range(FC):
                        i = te.matmul(sb["mx_pbc"][0:32, :], sb["mx_xpb"][:, f * 160 + 128:(f + 1) * 160],
                                      sb["mx_cx"][:, f * TC2:(f + 1) * TC2], start=(f == 0), stop=(f == FC - 1))
                    p.tag(i, "pe_sp")

                @blk.scalar
                def _(sc, h=h, col0=col0):
                    p.wait(sc, "pe_sp")
                    p.wait(sc, "pe_dt")
                    p.tag(sc.activation(out=sb["mx_dtin"][:], in_=sb["mx_pdtin"], func=AF.Copy), "s_spev")

                @blk.vector
                def _(v, h=h, col0=col0):
                    p.wait(v, "pe_sp")
                    p.wait(v, "q_bc")
                    p.tag(v.tensor_copy(out=sb["mx_bc"][:], in_=sb["mx_pbc"][0:32, :]), "v_bcev")

                @blk.sync
                def _(s, h=h, col0=col0):
                    p.wait(s, "v_bcev")
                    p.dma(s, "q_bc", dt["bc_s"][h, :, col0:col0 + TC2], sb["mx_bc"][:])

                for dc in range(FC):
                    @blk.tensor
                    def _(te, h=h, dc=dc, col0=col0):
                        p.wait(te, "s_spev")
                        p.wait(te, "s_dtp")
                        p.tag(te.matmul(sb["mx_pdt"], sb["mx_dtT"][:, dc * 128:(dc + 1) * 128],
                                        sb["mx_dtin"][:], start=True, stop=True), "pe_dt")

                    @blk.scalar
                    def _(sc, h=h, dc=dc, col0=col0):
                        p.wait(sc, "pe_dt")
                        p.wait(sc, "v_dtp")
                        bcol = sb["mx_dtb"][:, h * FC + dc: h * FC + dc + 1]
                        p.chain(sc, sc.activation(out=sb["mx_sp1"][:], in_=sb["mx_pdt"], func=AF.Abs,
                                                  bias=bcol))
                        p.chain(sc, sc.activation(out=sb["mx_sp1"][:], in_=sb["mx_sp1"][:], func=AF.Exp,
                                                  scale=-1.0))
                        p.chain(sc, sc.activation(out=sb["mx_sp1"][:], in_=sb["mx_sp1"][:], func=AF.Ln,
                                                  bias=1.0))
                        p.tag(sc.activation(out=sb["mx_sp2"][:], in_=sb["mx_pdt"], func=AF.Relu,
                                            bias=bcol), "s_dtp")

                    @blk.vector
                    def _(v, h=h, dc=dc, col0=col0):
                        p.wait(v, "s_dtp")
                        p.wait(v, "q_dt")
                        p.tag(v.tensor_tensor(out=sb["mx_dtev"][:], in0=sb["mx_sp1"][:],
                                              in1=sb["mx_sp2"][:], op=ALU.add), "v_dtp")

                    @blk.sync
                    def _(s, h=h, dc=dc, col0=col0):
                        p.wait(s, "v_dtp")
                        p.dma(s, "q_dt", dt["dt_s"][h * HD + dc * 128: h * HD + (dc + 1) * 128,
                                                    col0:col0 + TC2], sb["mx_dtev"][:])



def _scan_alloc(p, dt, sb, S_, P_):
    TCS = 512
    S_("sc_A", [128, NH * FC * NS], F32)
    S_("sc_dt", [128, 2 * TCS], BF16)
    S_("sc_cx", [128, 2 * TCS], BF16)
    S_("sc_gt", [128, 2 * TCS], BF16)
    S_("sc_u", [128, 2 * TCS], F32)
    S_("sc_a", [128, 2 * TCS], F32)
    S_("sc_b", [128, 2 * TCS], F32)
    S_("sc_s", [128, NS * TCS], F32)
    S_("sc_bcB", [128, NS * TCS], BF16)
    S_("sc_bcC", [128, NS * TCS], BF16)
    S_("sc_carry", [128, FC * NS], F32)
    S_("sc_c1b", [128, 1], F32)
    S_("sc_c1s", [128, FC], F32)
    S_("sc_D", [128, 32], F32)
    S_("sc_y", [128, TCS], F32)
    S_("sc_y2", [128, 2 * TCS], BF16)
    S_("sc_sum", [128, 2048], BF16)
    S_("sc_c1", [128, 1], F32)
    S_("sc_c16", [128, NS], F32)


def _scan_A_prep(p, dt, sb):
    blk = p.block

    @blk.sync
    def _(s):
        p.dma(s, "q_A", sb["sc_A"][:].rearrange("p (h t n) -> p h t n", h=NH, t=FC),
              dt["Alog"].rearrange("h (t p) n -> p h t n", p=128))
        p.dma(s, "q_A", sb["sc_D"][:], dt["Dmix"])

    @blk.scalar
    def _(sc):
        p.wait(sc, "q_A")
        p.tag(sc.activation(out=sb["sc_A"][:], in_=sb["sc_A"][:], func=AF.Exp), "s_A")

    @blk.vector
    def _(v):
        p.wait(v, "s_A")
        p.tag(v.tensor_scalar(out=sb["sc_A"][:], in0=sb["sc_A"][:], scalar1=-1.0, scalar2=None,
                              op0=ALU.mult), "v_A")


def ph_scan1(p, dt, sb, S_, P_):
    blk = p.block
    TCS = 512
    _scan_alloc(p, dt, sb, S_, P_)
    _scan_A_prep(p, dt, sb)

    for h in range(NH):
        for bs in range(2):
            for t2 in range(2):
                col0 = bs * BLK + t2 * TCS

                @blk.gpsimd
                def _(s, h=h, col0=col0):
                    p.wait(s, "v_scan")
                    for n in range(NS):
                        p.dma(s, "q_bcb", sb["sc_bcB"][:, n * TCS:(n + 1) * TCS],
                              dt["bc_s"][h, n:n + 1, col0:col0 + TCS].partition_broadcast(128))

                for dtile in range(FC):
                    dg = h * FC + dtile
                    tp_ = dtile % 2
                    DT = sb["sc_dt"][:, tp_ * TCS:(tp_ + 1) * TCS]
                    CX = sb["sc_cx"][:, tp_ * TCS:(tp_ + 1) * TCS]
                    UU = sb["sc_u"][:, tp_ * TCS:(tp_ + 1) * TCS]
                    QS = "q_sc0" if tp_ == 0 else "q_sc1"

                    @blk.sync
                    def _(s, dg=dg, col0=col0, DT=DT, CX=CX, QS=QS):
                        p.wait(s, "v_u", p.lo("v_u", 1))
                        p.wait(s, "s_scana", p.lo("s_scana", NS))
                        p.wait(s, "v_scan", p.lo("v_scan", NS))
                        p.dma(s, QS, DT, dt["dt_s"][dg * 128:(dg + 1) * 128, col0:col0 + TCS])
                        p.dma(s, QS, CX, dt["convx_s"][dg * 128:(dg + 1) * 128, col0:col0 + TCS])

                    @blk.vector
                    def _(v, dg=dg, dtile=dtile, t2=t2, bs=bs, DT=DT, CX=CX, UU=UU, QS=QS):
                        p.wait(v, QS)
                        p.wait(v, "q_bcb")
                        p.wait(v, "v_A")
                        p.chain(v, v.tensor_tensor(out=UU, in0=DT, in1=CX, op=ALU.mult))
                        p.chain(v, v.tensor_reduce(out=sb["sc_c1"][:] if t2 == 0 else sb["sc_c1b"][:],
                                                   in_=DT, axis=mybir.AxisListType.X,
                                                   op=ALU.add))
                        p.tag(v.nop(), "v_u")

                    for n in range(NS):
                        par = n % 2
                        A_ = sb["sc_a"][:, par * TCS:(par + 1) * TCS]
                        B_ = sb["sc_b"][:, par * TCS:(par + 1) * TCS]

                        @blk.scalar
                        def _(sc, dg=dg, n=n, A_=A_, DT=DT):
                            p.wait(sc, "v_u")
                            p.wait(sc, "v_scan", p.lo("v_scan", 1))
                            p.tag(sc.activation(out=A_, in_=DT, func=AF.Exp,
                                                scale=sb["sc_A"][:, dg * NS + n: dg * NS + n + 1]), "s_scana")

                        @blk.vector
                        def _(v, dg=dg, dtile=dtile, n=n, t2=t2, A_=A_, B_=B_, UU=UU):
                            p.wait(v, "s_scana")
                            p.chain(v, v.tensor_tensor(out=B_, in0=UU,
                                                       in1=sb["sc_bcB"][:, n * TCS:(n + 1) * TCS], op=ALU.mult))
                            init = (0.0 if t2 == 0
                                    else sb["sc_carry"][:, dtile * NS + n: dtile * NS + n + 1])
                            p.tag(v.tensor_tensor_scan(out=sb["sc_s"][:, n * TCS:(n + 1) * TCS],
                                                       data0=A_, data1=B_,
                                                       initial=init, op0=ALU.mult, op1=ALU.add), "v_scan")

                    @blk.vector
                    def _(v, dg=dg, dtile=dtile, bs=bs, t2=t2):
                        p.wait(v, "v_scan")
                        if t2 == 0:
                            p.chain(v, v.tensor_copy(out=sb["sc_carry"][:, dtile * NS:(dtile + 1) * NS],
                                                     in_=sb["sc_s"][:, TCS - 1::TCS]))
                            p.chain(v, v.tensor_copy(out=sb["sc_c1s"][:, dtile:dtile + 1], in_=sb["sc_c1"][:]))
                        else:
                            p.chain(v, v.tensor_copy(out=sb["sc_sum"][:, (bs * 2 + 0) * 512 + dg * NS:
                                                                      (bs * 2 + 0) * 512 + (dg + 1) * NS],
                                                     in_=sb["sc_s"][:, TCS - 1::TCS]))
                            p.chain(v, v.tensor_tensor(out=sb["sc_c1"][:], in0=sb["sc_c1b"][:],
                                                       in1=sb["sc_c1s"][:, dtile:dtile + 1], op=ALU.add))
                            p.chain(v, v.tensor_scalar(out=sb["sc_c16"][:], in0=sb["sc_A"][:, dg * NS:(dg + 1) * NS],
                                                       scalar1=sb["sc_c1"][:], scalar2=None, op0=ALU.mult))
                            p.tag(v.nop(), "v_c16")

                    if t2 == 1:
                        @blk.scalar
                        def _(sc, dg=dg, bs=bs):
                            p.wait(sc, "v_c16")
                            p.tag(sc.activation(out=sb["sc_sum"][:, (bs * 2 + 1) * 512 + dg * NS:
                                                                 (bs * 2 + 1) * 512 + (dg + 1) * NS],
                                                in_=sb["sc_c16"][:], func=AF.Exp), "s_sumD")

    @blk.sync
    def _(s):
        p.wait(s, "s_sumD")
        p.wait(s, "v_scan")
        p.dma(s, "q_sum", dt["sum_in"], sb["sc_sum"][:])


def ph_combine(p, dt, sb, S_, P_):
    blk = p.block
    S_("cb_sd", [128, 32 * 512], BF16)
    S_("cb_si", [128, 512], F32)
    S_("cb_st", [128, 2 * 512], F32)
    S_("cb_sel", [128, 32], F32)
    S_("cb_tmp", [128, 512], F32)

    @blk.gpsimd
    def _(g):
        p.wait(g, "q_sum")
        i = g.collective_compute("AllGather", ALU.bypass,
                                 replica_groups=[list(range(8))],
                                 ins=[dt["sum_in"].opt()], outs=[dt["sum_out"].opt()])
        p.tag(i, "cc_sum")

    @blk.sync
    def _(s):
        p.wait(s, "cc_sum")
        for idx in range(16):
            b_, pos = idx // 8, idx % 8
            r = (pos if pos < 4 else 7 - pos) + 4 * b_
            slot = 0 if pos < 4 else 1
            p.dma(s, "q_cb", sb["cb_sd"][:, (2 * idx) * 512:(2 * idx + 1) * 512],
                  dt["sum_out"][r * 128:(r + 1) * 128, (slot * 2) * 512:(slot * 2 + 1) * 512])
            p.dma(s, "q_cb", sb["cb_sd"][:, (2 * idx + 1) * 512:(2 * idx + 2) * 512],
                  dt["sum_out"][r * 128:(r + 1) * 128, (slot * 2 + 1) * 512:(slot * 2 + 2) * 512])
        p.dma(s, "q_cb", sb["cb_sel"][:, 0:16], dt["selA"].partition_broadcast(128))
        p.dma(s, "q_cb", sb["cb_sel"][:, 16:32], dt["selB"].partition_broadcast(128))

    @blk.vector
    def _(v):
        p.wait(v, "q_cb")
        p.chain(v, v.memset(sb["cb_st"][:], 0.0))
        for b_ in range(2):
            p.chain(v, v.memset(sb["cb_si"][:], 0.0))
            for pos in range(8):
                idx = b_ * 8 + pos
                p.chain(v, v.scalar_tensor_tensor(out=sb["cb_st"][:, 0:512], in0=sb["cb_si"][:],
                                                  scalar=sb["cb_sel"][:, idx:idx + 1],
                                                  in1=sb["cb_st"][:, 0:512], op0=ALU.mult, op1=ALU.add))
                p.chain(v, v.scalar_tensor_tensor(out=sb["cb_st"][:, 512:1024], in0=sb["cb_si"][:],
                                                  scalar=sb["cb_sel"][:, 16 + idx:17 + idx],
                                                  in1=sb["cb_st"][:, 512:1024], op0=ALU.mult, op1=ALU.add))
                p.chain(v, v.tensor_tensor(out=sb["cb_tmp"][:], in0=sb["cb_si"][:],
                                           in1=sb["cb_sd"][:, (2 * idx + 1) * 512:(2 * idx + 2) * 512],
                                           op=ALU.mult))
                p.chain(v, v.tensor_tensor(out=sb["cb_si"][:], in0=sb["cb_tmp"][:],
                                           in1=sb["cb_sd"][:, (2 * idx) * 512:(2 * idx + 1) * 512],
                                           op=ALU.add))
        p.tag(v.nop(), "v_cb")

    @blk.sync
    def _(s):
        p.wait(s, "v_cb")
        p.dma(s, "q_st", dt["stin_s"][0], sb["cb_st"][:, 0:512])
        p.dma(s, "q_st", dt["stin_s"][1], sb["cb_st"][:, 512:1024])


def ph_scan2(p, dt, sb, S_, P_):
    blk = p.block
    TCS = 512
    _scan_alloc(p, dt, sb, S_, P_)
    _scan_A_prep(p, dt, sb)
    S_("sc_stin", [128, FC * NS], F32)

    for h in range(NH):
        for bs in range(2):
            @blk.sync
            def _(s, h=h, bs=bs):
                p.wait(s, "v_scan")
                p.dma(s, "q_si", sb["sc_stin"][:], dt["stin_s"][bs, :, h * FC * NS:(h + 1) * FC * NS])

            for t2 in range(2):
                col0 = bs * BLK + t2 * TCS

                @blk.gpsimd
                def _(s, h=h, col0=col0):
                    p.wait(s, "v_scan")
                    for n in range(NS):
                        p.dma(s, "q_bcb", sb["sc_bcB"][:, n * TCS:(n + 1) * TCS],
                              dt["bc_s"][h, n:n + 1, col0:col0 + TCS].partition_broadcast(128))
                    p.wait(s, "v_y2")
                    for n in range(NS):
                        p.dma(s, "q_bcb", sb["sc_bcC"][:, n * TCS:(n + 1) * TCS],
                              dt["bc_s"][h, NS + n:NS + n + 1, col0:col0 + TCS].partition_broadcast(128))

                for dtile in range(FC):
                    dg = h * FC + dtile
                    tp_ = dtile % 2
                    DT = sb["sc_dt"][:, tp_ * TCS:(tp_ + 1) * TCS]
                    CX = sb["sc_cx"][:, tp_ * TCS:(tp_ + 1) * TCS]
                    GT = sb["sc_gt"][:, tp_ * TCS:(tp_ + 1) * TCS]
                    UU = sb["sc_u"][:, tp_ * TCS:(tp_ + 1) * TCS]
                    Y2 = sb["sc_y2"][:, tp_ * TCS:(tp_ + 1) * TCS]
                    QS = "q_sc0" if tp_ == 0 else "q_sc1"
                    QY = "q_y20" if tp_ == 0 else "q_y21"

                    @blk.sync
                    def _(s, dg=dg, col0=col0, DT=DT, CX=CX, GT=GT, QS=QS, QY=QY):
                        p.wait(s, "v_u", p.lo("v_u", 1))
                        p.wait(s, "s_scana", p.lo("s_scana", NS))
                        p.wait(s, "v_y2", p.lo("v_y2", 1))
                        p.wait(s, QY)
                        p.dma(s, QS, DT, dt["dt_s"][dg * 128:(dg + 1) * 128, col0:col0 + TCS])
                        p.dma(s, QS, CX, dt["convx_s"][dg * 128:(dg + 1) * 128, col0:col0 + TCS])
                        p.dma(s, QS, GT, dt["gate_s"][dg * 128:(dg + 1) * 128, col0:col0 + TCS])

                    @blk.vector
                    def _(v, dg=dg, DT=DT, CX=CX, UU=UU, QS=QS):
                        p.wait(v, QS)
                        p.wait(v, "q_bcb")
                        p.wait(v, "v_A")
                        p.wait(v, "q_si")
                        p.tag(v.tensor_tensor(out=UU, in0=DT, in1=CX, op=ALU.mult), "v_u")

                    for n in range(NS):
                        par = n % 2
                        A_ = sb["sc_a"][:, par * TCS:(par + 1) * TCS]
                        B_ = sb["sc_b"][:, par * TCS:(par + 1) * TCS]

                        @blk.scalar
                        def _(sc, dg=dg, n=n, A_=A_, DT=DT):
                            p.wait(sc, "v_u")
                            p.wait(sc, "v_scan", p.lo("v_scan", 1))
                            p.tag(sc.activation(out=A_, in_=DT, func=AF.Exp,
                                                scale=sb["sc_A"][:, dg * NS + n: dg * NS + n + 1]), "s_scana")

                        @blk.vector
                        def _(v, dg=dg, dtile=dtile, n=n, t2=t2, A_=A_, B_=B_, UU=UU):
                            p.wait(v, "s_scana")
                            p.chain(v, v.tensor_tensor(out=B_, in0=UU,
                                                       in1=sb["sc_bcB"][:, n * TCS:(n + 1) * TCS], op=ALU.mult))
                            init = (sb["sc_stin"][:, dtile * NS + n: dtile * NS + n + 1] if t2 == 0
                                    else sb["sc_carry"][:, dtile * NS + n: dtile * NS + n + 1])
                            p.tag(v.tensor_tensor_scan(out=sb["sc_s"][:, n * TCS:(n + 1) * TCS],
                                                       data0=A_, data1=B_,
                                                       initial=init, op0=ALU.mult, op1=ALU.add), "v_scan")

                    @blk.vector
                    def _(v, dg=dg, dtile=dtile, t2=t2, bs=bs, CX=CX, GT=GT, Y2=Y2):
                        p.wait(v, "v_scan")
                        if t2 == 0:
                            p.chain(v, v.tensor_copy(out=sb["sc_carry"][:, dtile * NS:(dtile + 1) * NS],
                                                     in_=sb["sc_s"][:, TCS - 1::TCS]))
                        for n in range(NS):
                            i = v.tensor_tensor(out=sb["sc_s"][:, n * TCS:(n + 1) * TCS],
                                                in0=sb["sc_s"][:, n * TCS:(n + 1) * TCS],
                                                in1=sb["sc_bcC"][:, n * TCS:(n + 1) * TCS], op=ALU.mult)
                        p.chain(v, i)
                        step = 1
                        while step < NS:
                            for n in range(0, NS, 2 * step):
                                i = v.tensor_tensor(out=sb["sc_s"][:, n * TCS:(n + 1) * TCS],
                                                    in0=sb["sc_s"][:, n * TCS:(n + 1) * TCS],
                                                    in1=sb["sc_s"][:, (n + step) * TCS:(n + step + 1) * TCS],
                                                    op=ALU.add)
                            p.chain(v, i)
                            step *= 2
                        p.chain(v, v.scalar_tensor_tensor(out=sb["sc_y"][:], in0=CX,
                                                          scalar=sb["sc_D"][:, dg:dg + 1],
                                                          in1=sb["sc_s"][:, 0:TCS],
                                                          op0=ALU.mult, op1=ALU.add))
                        p.tag(v.tensor_tensor(out=Y2, in0=sb["sc_y"][:],
                                              in1=GT, op=ALU.mult), "v_y2")

                    @blk.sync
                    def _(s, dg=dg, col0=col0, Y2=Y2, QY=QY):
                        p.wait(s, "v_y2")
                        p.dma(s, QY, dt["y2_s"][dg * 128:(dg + 1) * 128, col0:col0 + TCS], Y2)



def ph_outproj(p, dt, sb, S_, P_):
    blk = p.block
    TC2 = 512
    S_("op_w32", [128, 2 * 32 * 128], F32)
    S_("op_wb", [128, 2 * 32 * 128], BF16)
    S_("op_y2", [128, 32 * TC2], BF16)
    S_("op_x2", [128, 2 * TC2], F32)
    S_("op_ev", [128, 2 * TC2], F32)
    P_("op_ps", [128, 2 * TC2])

    for t in range(TOK // TC2):
        @blk.sync
        def _(s, t=t):
            p.wait(s, "pe_op")
            p.dma(s, "q_oy", sb["op_y2"][:].rearrange("p (i s) -> p i s", i=32),
                  dt["y2_s"].rearrange("(i p) s -> p i s", p=128)[:, :, t * TC2:(t + 1) * TC2])

        for o in range(FC):
            wp = o % 2
            W32 = sb["op_w32"][:, wp * 4096:(wp + 1) * 4096]
            WB = sb["op_wb"][:, wp * 4096:(wp + 1) * 4096]

            @blk.gpsimd
            def _(s, t=t, o=o, W32=W32):
                p.wait(s, "c_opw", p.lo("c_opw", 1))
                p.dma(s, "q_ow", W32.rearrange("p (i d) -> p i d", i=32),
                      dt["outT"].rearrange("(i p) d -> p i d", p=128)[:, :, o * 128:(o + 1) * 128])

            @blk.scalar
            def _(sc, t=t, o=o, W32=W32, WB=WB):
                p.wait(sc, "q_ow")
                p.wait(sc, "pe_op", p.lo("pe_op", 1))
                p.tag(sc.activation(out=WB, in_=W32, func=AF.Copy), "c_opw")

            @blk.tensor
            def _(te, t=t, o=o, WB=WB, wp=wp):
                p.wait(te, "c_opw")
                p.wait(te, "q_oy")
                p.wait(te, "v_opev", p.lo("v_opev", 1))
                for ic in range(32):
                    i = te.matmul(sb["op_ps"][:, wp * TC2:(wp + 1) * TC2], WB[:, ic * 128:(ic + 1) * 128],
                                  sb["op_y2"][:, ic * TC2:(ic + 1) * TC2],
                                  start=(ic == 0), stop=(ic == 31))
                p.tag(i, "pe_op")

            @blk.sync
            def _(s, t=t, o=o, wp=wp):
                p.wait(s, "v_opev", p.lo("v_opev", 1))
                p.dma(s, "q_ox", sb["op_x2"][:, wp * TC2:(wp + 1) * TC2],
                      dt["x2T_s"][o * 128:(o + 1) * 128, t * TC2:(t + 1) * TC2])

            @blk.vector
            def _(v, t=t, o=o, wp=wp):
                p.wait(v, "pe_op")
                p.wait(v, "q_ox")
                p.wait(v, "q_x3")
                p.tag(v.tensor_tensor(out=sb["op_ev"][:, wp * TC2:(wp + 1) * TC2],
                                      in0=sb["op_ps"][:, wp * TC2:(wp + 1) * TC2],
                                      in1=sb["op_x2"][:, wp * TC2:(wp + 1) * TC2], op=ALU.add), "v_opev")

            @blk.sync
            def _(s, t=t, o=o, wp=wp):
                p.wait(s, "v_opev")
                p.dma(s, "q_x3", dt["x3T_s"][o * 128:(o + 1) * 128, t * TC2:(t + 1) * TC2],
                      sb["op_ev"][:, wp * TC2:(wp + 1) * TC2])




_CACHE = {}


def kernel(**inputs):
    pr = host_prepare(inputs)
    key = (DEBUG_PHASES, tuple(DEBUG_DUMPS))
    if key not in _CACHE:
        _CACHE[key] = build_program(DEBUG_PHASES, tuple(DEBUG_DUMPS))
    nc, dtens = _CACHE[key]
    in_maps = [per_core_inputs(c, pr) for c in range(8)]
    res = run_bass_kernel_spmd(nc, in_maps, core_ids=list(range(8)))
    return host_assemble(res)


def host_assemble(res):
    out = np.empty((B, S, E), dtype=np.float32)
    for c in range(8):
        b, q = c // 4, c % 4
        o = np.asarray(res.results[c]["out"])
        bA, bB = q, 7 - q
        out[b, bA * BLK:(bA + 1) * BLK] = o[:, 0:BLK].T
        out[b, bB * BLK:(bB + 1) * BLK] = o[:, BLK:2 * BLK].T
    return out


def run_debug(inputs, phases, dumps):
    global DEBUG_PHASES, DEBUG_DUMPS
    DEBUG_PHASES, DEBUG_DUMPS = phases, tuple(dumps)
    try:
        pr = host_prepare(inputs)
        nc, dtens = build_program(DEBUG_PHASES, DEBUG_DUMPS)
        in_maps = [per_core_inputs(c, pr) for c in range(8)]
        res = run_bass_kernel_spmd(nc, in_maps, core_ids=list(range(8)))
        return res
    finally:
        DEBUG_PHASES, DEBUG_DUMPS = 99, ()
